# version 47
# baseline (speedup 1.0000x reference)
import numpy as np
import concourse.bass as bass
import concourse.mybir as mybir
from concourse.bass_utils import run_bass_kernel_spmd
from contextlib import ExitStack

F32 = mybir.dt.float32
F32R = mybir.dt.float32r
BF16 = mybir.dt.bfloat16
ALU = mybir.AluOpType
ACT = mybir.ActivationFunctionType

D = 1024
N = 4096
NCH = 8
NJ = 32
T = 512
NW = N // T
EW = 256
NEP = D // EW
CTX_SCALE = 0.125 / 1024.0
EPS = 1e-5


class Sem:
    def __init__(self, es, nc, name):
        self.h = es.enter_context(nc.semaphore(name))
        self.n = 0


def build_nc(do_a2=True, b_windows=NW, dbg_store=None):
    nc = bass.Bass("TRN2", target_bir_lowering=False)

    xT = nc.dram_tensor("xT", [D, N], F32, kind="ExternalInput").ap()
    wq_h = nc.dram_tensor("wq", [D, D], F32, kind="ExternalInput").ap()
    wk_h = nc.dram_tensor("wk", [D, D], F32, kind="ExternalInput").ap()
    wv_h = nc.dram_tensor("wv", [D, D], F32, kind="ExternalInput").ap()
    wo_h = nc.dram_tensor("wo", [D, D], F32, kind="ExternalInput").ap()
    wrT_h = nc.dram_tensor("wrT", [D, D], F32, kind="ExternalInput").ap()
    wkT_h = nc.dram_tensor("wkT", [D, D], F32, kind="ExternalInput").ap()
    wvT_h = nc.dram_tensor("wvT", [D, D], F32, kind="ExternalInput").ap()
    bo_h = nc.dram_tensor("bo", [D], F32, kind="ExternalInput").ap()
    mur_h = nc.dram_tensor("mur", [D], F32, kind="ExternalInput").ap()
    muk_h = nc.dram_tensor("muk", [D], F32, kind="ExternalInput").ap()
    outT = nc.dram_tensor("outT", [D, N], F32, kind="ExternalOutput").ap()
    wbf_scr = {t: nc.dram_tensor(f"wscr_{t}", [D, D], BF16, kind="Internal").ap()
               for t in ("wq", "wo", "wr", "wk", "wv")}

    def _szb(shape, dtype):
        n = 1
        for s in shape[1:]:
            n *= s
        n *= mybir.dt.size(dtype)
        return (n + 31) & ~31

    spec_p = {
        "ctx_bf": ([128, NCH * D], BF16),
        "ones_bf": ([128, 128], BF16),
        "ones_col": ([128, 1], BF16),
        "sinv": ([128, 8], F32),
        "bo_sb": ([128, 8], F32),
        "c_m1": ([128, 1], F32),
        "c_mh": ([128, 1], F32),
        "c_eps": ([128, 1], F32),
        "c_lnsc": ([128, 1], F32),
        "mur_sb": ([128, 8], F32),
        "muk_sb": ([128, 8], F32),
    }
    spec_a = {
        "x_bf": ([128, NCH * N], BF16),
        "kexp": ([128, NJ * D], BF16),
        "v_sb": ([128, NJ * EW], BF16),
        "wk_bf": ([128, NCH * D], BF16),
        "wv_bf": ([128, NCH * D], BF16),
        "sd_acc": ([128, D], F32),
        "sd_acc2": ([128, D], F32),
        "sd_hi": ([128, D], BF16),
        "sd_lo": ([128, D], BF16),
    }
    spec_b = {
        "wslot0": ([128, NCH * D], BF16),
        "wslot1": ([128, NCH * D], BF16),
        "xw0": ([128, NCH * T], F32),
        "xw1": ([128, NCH * T], F32),
        "xw2": ([128, NCH * T], F32),
        "qexp": ([128, NCH * T], BF16),
        "y_bf": ([128, NCH * T], BF16),
        "u1_bf": ([128, NCH * T], BF16),
        "usq": ([128, NCH * T], BF16),
        "hbuf": ([128, NCH * (T + 1)], F32),
        "r_w": ([128, NCH * T], BF16),
        "xr": ([128, NCH * T], BF16),
        "xk": ([128, NCH * T], BF16),
        "ka": ([128, NCH * T], BF16),
        "dd": ([128, T], F32),
        "t3": ([128, T], F32),
        "srec": ([128, T], F32),
        "st_mean": ([128, T], F32),
        "st_var": ([128, T], F32),
        "st_rstd": ([128, T], F32),
        "st_mr": ([128, T], F32),
        "xbfw": ([128, NCH * T], BF16),
        "xbfw1": ([128, NCH * T], BF16),
    }
    p_bytes = sum(_szb(*v) for v in spec_p.values())
    a_bytes = sum(_szb(*v) for v in spec_a.values())
    b_bytes = sum(_szb(*v) for v in spec_b.values())
    total = p_bytes + max(a_bytes, b_bytes)
    base = nc.alloc_sbuf_tensor("arena", [128, total], mybir.dt.uint8)
    addr0 = nc.lookup_mloc(base).addr
    H = {}
    off = addr0
    for nm, (shape, dt_) in spec_p.items():
        H[nm] = nc.alloc_sbuf_tensor_at(nm, shape, dt_, offset=off)
        off += _szb(shape, dt_)
    ov0 = off
    for spec in (spec_a, spec_b):
        off = ov0
        for nm, (shape, dt_) in spec.items():
            H[nm] = nc.alloc_sbuf_tensor_at(nm, shape, dt_, offset=off)
            off += _szb(shape, dt_)
    ctx_bf = H["ctx_bf"]; ones_bf = H["ones_bf"]; ones_col = H["ones_col"]
    sinv = H["sinv"]; bo_sb = H["bo_sb"]; mur_sb = H["mur_sb"]; muk_sb = H["muk_sb"]
    c_m1 = H["c_m1"]; c_mh = H["c_mh"]; c_eps = H["c_eps"]; c_lnsc = H["c_lnsc"]
    x_bf = H["x_bf"]; kexp = H["kexp"]; v_sb = H["v_sb"]
    wk_bf = H["wk_bf"]; wv_bf = H["wv_bf"]; sd_acc = H["sd_acc"]
    sd_acc2 = H["sd_acc2"]; sd_hi = H["sd_hi"]; sd_lo = H["sd_lo"]
    wslot = [H["wslot0"], H["wslot1"]]
    xw = [H["xw0"], H["xw1"], H["xw2"]]
    xbfw = [H["xbfw"], H["xbfw1"]]; qexp = H["qexp"]; y_bf = H["y_bf"]
    u1_bf = H["u1_bf"]; usq = H["usq"]; hbuf = H["hbuf"]; r_w = H["r_w"]
    xr = H["xr"]; xk = H["xk"]; ka = H["ka"]
    dd = H["dd"]; t2 = H["dd"]; t3 = H["t3"]; dd2 = H["t3"]; srec = H["srec"]
    st_mean = H["st_mean"]; st_tmp = H["t3"]; st_var = H["st_var"]
    st_rstd = H["st_rstd"]; st_mr = H["st_mr"]

    pb = [nc.alloc_psum_tensor(f"pb{i}", [128, 512], F32) for i in range(8)]

    TT, VV, SS, SY, GP = [], [], [], [], []
    streams = {"t": TT, "v": VV, "s": SS, "y": SY, "g": GP}

    def wait(st, sem, v):
        if v > 0:
            streams[st].append(lambda e, s=sem.h, vv=v: e.wait_ge(s, vv))

    def op(st, meth, *args, inc=None, inck=1, **kw):
        if inc is not None:
            inc.n += inck
            tgt = (inc.h, inck)
        else:
            tgt = None

        def f(e, m=meth, a=args, k=kw, t=tgt):
            inst = getattr(e, m)(*a, **k)
            if t is not None:
                inst.then_inc(*t)
        streams[st].append(f)

    es = ExitStack()
    sem_names = [
        "ldm", "xbf", "wa", "wav", "kmm", "kev", "sda", "sdm", "siv", "vmm",
        "vev", "cmm", "cev", "wld", "sxw", "sxb", "qmm", "qev", "smm", "sre",
        "ymm", "yev", "womm", "su1", "usq_s", "u1b", "s1m", "s2m", "siv2",
        "srs", "chn", "mix", "rmm", "rsg", "kfm", "kfr", "ka_s", "owm", "owe",
        "out_s", "sto", "ini", "sdh", "mixg", "mixc",
        "wc_wq", "wc_wo", "wc_wr", "wc_wk", "wc_wv", "vsf", "ssf",
    ]
    S = {n: Sem(es, nc, n) for n in sem_names}

    bank_rel = [None] * 8

    def acquire(st, *banks):
        for b in banks:
            if bank_rel[b] is not None:
                sem, v = bank_rel[b]
                wait(st, sem, v)

    def release(b, sem, v):
        bank_rel[b] = (sem, v)

    def dma(st, out_ap, in_ap, inc, **kw):
        op(st, "dma_start", inc=inc, inck=16, out=out_ap, in_=in_ap, **kw)

    dma("y", bo_sb[:], bo_h.rearrange("(c p) -> p c", p=128), S["ldm"],
        allow_slow_non_contiguous=True)
    dma("y", mur_sb[:], mur_h.rearrange("(c p) -> p c", p=128), S["ldm"],
        allow_slow_non_contiguous=True)
    dma("y", muk_sb[:], muk_h.rearrange("(c p) -> p c", p=128), S["ldm"],
        allow_slow_non_contiguous=True)

    op("v", "memset", ones_bf[:], 1.0, inc=S["ini"])
    op("v", "memset", ones_col[:], 1.0, inc=S["ini"])
    op("v", "memset", c_m1[:], -1.0)
    op("v", "memset", c_mh[:], -0.5)
    op("v", "memset", c_eps[:], EPS)
    op("v", "memset", c_lnsc[:], float(np.log(CTX_SCALE)), inc=S["ini"])

    def wchunk_load(dst, src_h, sem):
        for c in range(NCH):
            dma("g", dst[:, c * D:(c + 1) * D],
                src_h[c * 128:(c + 1) * 128, :], sem)

    wchunk_load(wk_bf, wk_h, S["wa"])
    for s_i in range(8):
        wait("g", S["xbf"], 128 * s_i)
        for c in range(NCH):
            dma("g", x_bf[:, c * N + 512 * s_i: c * N + 512 * (s_i + 1)],
                xT[c * 128:(c + 1) * 128, 512 * s_i:512 * (s_i + 1)], S["xbf"])
        if s_i == 0:
            wchunk_load(wv_bf, wv_h, S["wav"])
    wsrc_f32 = {"wq": wq_h, "wo": wo_h, "wr": wrT_h, "wk": wkT_h, "wv": wvT_h}
    for t in ("wq", "wo", "wr", "wk", "wv"):
        dma("g", wbf_scr[t][:, :], wsrc_f32[t][:, :], S["wc_" + t])

    for j in range(NJ):
        p = j % 2
        b0, b1 = 2 * p, 2 * p + 1
        s_i = j // 4
        wait("t", S["xbf"], 128 * (s_i + 1))
        if j == 0:
            wait("t", S["wa"], 128)
        acquire("t", b0, b1)
        for i in range(NCH):
            st = (i == 0)
            sp = (i == NCH - 1)
            xtile = x_bf[:, i * N + 128 * j: i * N + 128 * (j + 1)]
            op("t", "matmul", pb[b0][:, 0:512], xtile,
               wk_bf[:, i * D + 0: i * D + 512], start=st, stop=sp)
            op("t", "matmul", pb[b1][:, 0:512], xtile,
               wk_bf[:, i * D + 512: i * D + 1024], start=st, stop=sp,
               inc=(S["kmm"] if (i == NCH - 1) else None))
        wait("s", S["kmm"], j + 1)
        op("s", "activation", kexp[:, j * D + 0: j * D + 512], pb[b0][:, 0:512],
           ACT.Exp, bias=0.0, scale=1.0)
        op("s", "activation", kexp[:, j * D + 512: j * D + 1024], pb[b1][:, 0:512],
           ACT.Exp, bias=0.0, scale=1.0, inc=S["kev"])
        release(b0, S["kev"], S["kev"].n)
        release(b1, S["kev"], S["kev"].n)
        wait("v", S["kev"], j + 1)
        acc = sd_acc if j % 2 == 0 else sd_acc2
        if j < 2:
            op("v", "tensor_copy", acc[:], kexp[:, j * D:(j + 1) * D],
               inc=S["sda"])
        else:
            wait("v", S["sda"], j - 1)
            op("v", "tensor_add", acc[:], acc[:],
               kexp[:, j * D:(j + 1) * D], inc=S["sda"])
    wait("v", S["sda"], NJ)
    op("v", "tensor_add", sd_acc[:], sd_acc[:], sd_acc2[:], inc=S["sda"])

    wait("s", S["sda"], NJ + 1)
    op("s", "activation", sd_hi[:], sd_acc[:], ACT.Copy, bias=0.0, scale=1.0,
       inc=S["sdh"])
    wait("v", S["sdh"], 1)
    op("v", "tensor_tensor", sd_lo[:], sd_acc[:], sd_hi[:], ALU.subtract,
       inc=S["sda"])

    for t_i in range(NEP if do_a2 else 0):
        e0 = EW * t_i
        if t_i == 0:
            wait("t", S["wav"], 128)
        for j in range(NJ):
            bk = 6 + (j % 2)
            acquire("t", bk)
            for i in range(NCH):
                st = (i == 0)
                sp = (i == NCH - 1)
                xtile = x_bf[:, i * N + 128 * j: i * N + 128 * (j + 1)]
                op("t", "matmul", pb[bk][:, 0:EW], xtile,
                   wv_bf[:, i * D + e0: i * D + e0 + EW], start=st, stop=sp,
                   inc=(S["vmm"] if i == NCH - 1 else None))
            wait("v", S["vmm"], NJ * t_i + j + 1)
            op("v", "tensor_copy", v_sb[:, j * EW:(j + 1) * EW], pb[bk][:, 0:EW],
               inc=S["vev"])
            release(bk, S["vev"], S["vev"].n)
        if t_i == 0:
            wait("t", S["sda"], NJ + 2)
            wait("t", S["ini"], 2)
            acquire("t", 4)
            for c in range(NCH):
                op("t", "matmul", pb[4][:, c:c + 1],
                   sd_hi[:, c * 128:(c + 1) * 128],
                   ones_col[:], start=True, stop=False)
                op("t", "matmul", pb[4][:, c:c + 1],
                   sd_lo[:, c * 128:(c + 1) * 128],
                   ones_col[:], start=False, stop=True,
                   inc=(S["sdm"] if c == NCH - 1 else None))
            wait("s", S["sdm"], 1)
            wait("s", S["ini"], 3)
            op("s", "activation", sinv[:], pb[4][:, 0:8], ACT.Ln, bias=0.0,
               scale=1.0, inc=S["ssf"])
            wait("s", S["ssf"], S["ssf"].n)
            op("s", "activation", sinv[:], sinv[:], ACT.Exp,
               bias=c_lnsc[:, 0:1], scale=c_m1[:, 0:1], inc=S["siv"])
            release(4, S["siv"], 1)
        wait("t", S["vev"], NJ * (t_i + 1))
        acquire("t", *range(8))
        for j in range(NJ):
            for c in range(NCH):
                op("t", "matmul", pb[c][:, 0:EW],
                   kexp[:, j * D + c * 128: j * D + (c + 1) * 128],
                   v_sb[:, j * EW:(j + 1) * EW],
                   start=(j == 0), stop=(j == NJ - 1),
                   inc=(S["cmm"] if (j == NJ - 1 and c == NCH - 1) else None))
        wait("s", S["cmm"], t_i + 1)
        if t_i == 0:
            wait("s", S["siv"], 1)
        for c in [6, 7, 0, 1, 2, 3, 4, 5]:
            op("s", "activation", ctx_bf[:, c * D + e0: c * D + e0 + EW],
               pb[c][:, 0:EW], ACT.Copy, bias=0.0, scale=sinv[:, c:c + 1],
               inc=S["cev"])
            release(c, S["cev"], S["cev"].n)

    wsrc = {"wq": wq_h, "wo": wo_h, "wr": wrT_h, "wk": wkT_h, "wv": wvT_h}
    wuse_rel = {}
    g_loaded = [0]
    g_pending = []
    wc_waited = set()

    def emit_wload(tag):
        g = g_loaded[0]
        if g >= 2:
            rel = wuse_rel[g - 2]
            wait("g", rel[0], rel[1])
        if g == 0:
            wait("g", S["vmm"], NJ * NEP)
        if tag not in wc_waited:
            wait("g", S["wc_" + tag], 16)
            wc_waited.add(tag)
        wait("g", S["wld"], 32 * g)
        src = wbf_scr[tag].rearrange("(k p) m -> p k m", p=128)
        dst = wslot[g % 2][:].rearrange("p (k m) -> p k m", k=NCH)
        for h in range(2):
            dma("g", dst[:, 4 * h:4 * (h + 1), :], src[:, 4 * h:4 * (h + 1), :],
                S["wld"])
        g_loaded[0] += 1
        return g

    def emit_xw_load(w):
        if w == 0:
            wait("y", S["vmm"], NJ * NEP)
        wait("y", S["sxw"], 128 * w)
        for c in range(NCH):
            dma("y", xw[w % 3][:, c * T:(c + 1) * T],
                xT[c * 128:(c + 1) * 128, T * w:T * (w + 1)], S["sxw"])

    NWL = b_windows
    if NWL > 0:
        stages = [("Q", 0), ("S", 0), ("y", 0), ("Wo", 0), ("LN1s", 0)]
        if NWL > 1:
            stages += [("Q", 1), ("S", 1), ("y", 1)]
        for w in range(NWL):
            stages += [("r", w), ("kf", w), ("ow", w), ("LN2s", w)]
            if w + 1 < NWL:
                stages += [("Wo", w + 1), ("LN1s", w + 1)]
            if w + 2 < NWL:
                stages += [("Q", w + 2), ("S", w + 2), ("y", w + 2)]
    else:
        stages = []

    cons_order = [(st, w) for (st, w) in stages if st in ("Q", "Wo", "r", "kf", "ow")]
    tag_of = {"Q": "wq", "Wo": "wo", "r": "wr", "kf": "wk", "ow": "wv"}
    g_of = {}
    for gi, (st, w) in enumerate(cons_order):
        g_of[(st, w)] = gi

    next_load = [2]
    xw_loaded = [2]
    if NWL > 0:
        emit_wload("wq")
        emit_wload("wo")
        emit_xw_load(0)
        if NWL > 1:
            emit_xw_load(1)

    def emit_convert(w):
        if w <= 1:
            wait("g", S["vmm"], NJ * NEP)
        else:
            wait("g", S["qmm"], 8 * (w - 2) + 8)
        wait("g", S["sxb"], 16 * w)
        dma("g", xbfw[w % 2][:].rearrange("p (c n) -> p c n", c=NCH),
            xT.rearrange("(c p) n -> p c n", p=128)[:, :, T * w:T * (w + 1)],
            S["sxb"])

    if NWL > 0:
        emit_convert(0)
        if NWL > 1:
            emit_convert(1)

    def stage_Q(w):
        g = g_of[("Q", w)]
        wait("t", S["wld"], 32 * (g + 1))
        wait("t", S["sxb"], 16 * (w + 1))
        if w >= 1:
            wait("s", S["ymm"], 8 * (w - 1) + 8)
        qbanks = [0, 1, 7, 2, 0, 1, 7, 2]
        for m in range(NCH):
            bk = qbanks[m]
            acquire("t", bk)
            for k in range(NCH):
                op("t", "matmul", pb[bk][:, 0:512],
                   wslot[g % 2][:, k * D + 128 * m: k * D + 128 * (m + 1)],
                   xbfw[w % 2][:, k * T:(k + 1) * T],
                   start=(k == 0), stop=(k == NCH - 1),
                   inc=(S["qmm"] if k == NCH - 1 else None))
            wait("s", S["qmm"], 8 * w + m + 1)
            op("s", "activation", qexp[:, m * T:(m + 1) * T], pb[bk][:, 0:512],
               ACT.Exp, bias=0.0, scale=1.0, inc=S["qev"])
            release(bk, S["qev"], S["qev"].n)
        wuse_rel[g] = (S["qmm"], 8 * w + 8)

    def stage_S(w):
        wait("t", S["qev"], 8 * w + 8)
        acquire("t", 2)
        for d_i in range(NCH):
            op("t", "matmul", pb[2][:, 0:512], ones_bf[:],
               qexp[:, d_i * T:(d_i + 1) * T],
               start=(d_i == 0), stop=(d_i == NCH - 1),
               inc=(S["smm"] if d_i == NCH - 1 else None))
        wait("s", S["smm"], w + 1)
        if w >= 1:
            wait("s", S["yev"], 8 * (w - 1) + 8)
        op("s", "activation", srec[:], pb[2][:, 0:512], ACT.Ln, bias=0.0,
           scale=1.0, inc=S["ssf"])
        wait("s", S["ssf"], S["ssf"].n)
        op("s", "activation", srec[:], srec[:], ACT.Exp, bias=0.0,
           scale=c_m1[:, 0:1], inc=S["sre"])
        release(2, S["sre"], S["sre"].n)

    def stage_y(w):
        if w == 0:
            wait("t", S["cev"], 8 * NEP)
        wait("v", S["sre"], w + 1)
        if w >= 1:
            wait("v", S["womm"], 8 * (w - 1) + 8)
        ybanks = [3, 4, 5, 6, 0, 1, 7, 2]
        for e in range(NCH):
            bk = ybanks[e]
            acquire("t", bk)
            for d_i in range(NCH):
                op("t", "matmul", pb[bk][:, 0:512],
                   ctx_bf[:, d_i * D + 128 * e: d_i * D + 128 * (e + 1)],
                   qexp[:, d_i * T:(d_i + 1) * T],
                   start=(d_i == 0), stop=(d_i == NCH - 1),
                   inc=(S["ymm"] if d_i == NCH - 1 else None))
            wait("v", S["ymm"], 8 * w + e + 1)
            op("v", "tensor_tensor", y_bf[:, e * T:(e + 1) * T],
               pb[bk][:, 0:512], srec[:], ALU.mult, inc=S["yev"])
            release(bk, S["yev"], S["yev"].n)

    def stage_Wo(w):
        g = g_of[("Wo", w)]
        ub = xw[w % 3]
        wait("t", S["wld"], 32 * (g + 1))
        wait("t", S["yev"], 8 * w + 8)
        if w == 0:
            wait("v", S["ldm"], 48)
        wait("v", S["sxw"], 128 * (w + 1))
        wobanks = [5, 6, 3, 4, 5, 6, 3, 4]
        for o in range(NCH):
            bk = wobanks[o]
            acquire("t", bk)
            for e in range(NCH):
                op("t", "matmul", pb[bk][:, 0:512],
                   wslot[g % 2][:, e * D + 128 * o: e * D + 128 * (o + 1)],
                   y_bf[:, e * T:(e + 1) * T],
                   start=(e == 0), stop=(e == NCH - 1),
                   inc=(S["womm"] if e == NCH - 1 else None))
            wait("v", S["womm"], 8 * w + o + 1)
            op("v", "scalar_tensor_tensor", ub[:, o * T:(o + 1) * T],
               pb[bk][:, 0:512], bo_sb[:, o:o + 1],
               ub[:, o * T:(o + 1) * T], ALU.add, ALU.add, inc=S["su1"])
            release(bk, S["su1"], S["su1"].n)
        wuse_rel[g] = (S["womm"], 8 * w + 8)

    def ln_mms(w, which):
        idx = 2 * w + which
        b1, b2 = (5, 6) if which == 1 else (3, 4)
        ub = xw[w % 3]
        src = S["su1"] if which == 1 else S["owe"]
        if idx >= 2:
            wait("s", S["s2m"], idx - 1)
            wait("s", S["s1m"], idx - 1)
        acquire("t", b1)
        acquire("t", b2)
        for d_i in range(NCH):
            wait("s", src, 8 * w + d_i + 1)
            op("s", "activation", usq[:, d_i * T:(d_i + 1) * T],
               ub[:, d_i * T:(d_i + 1) * T], ACT.Square, bias=0.0, scale=1.0,
               inc=S["usq_s"])
            op("s", "activation", u1_bf[:, d_i * T:(d_i + 1) * T],
               ub[:, d_i * T:(d_i + 1) * T], ACT.Copy, bias=0.0, scale=1.0,
               inc=S["u1b"])
            wait("t", S["u1b"], 8 * (idx - 1) + d_i + 1)
            op("t", "matmul", pb[b1][:, 0:512], ones_bf[:],
               u1_bf[:, d_i * T:(d_i + 1) * T],
               start=(d_i == 0), stop=(d_i == NCH - 1),
               inc=(S["s1m"] if d_i == NCH - 1 else None))
            wait("t", S["usq_s"], 8 * (idx - 1) + d_i + 1)
            op("t", "matmul", pb[b2][:, 0:512], ones_bf[:],
               usq[:, d_i * T:(d_i + 1) * T],
               start=(d_i == 0), stop=(d_i == NCH - 1),
               inc=(S["s2m"] if d_i == NCH - 1 else None))

    def ln_part1(w, which):
        idx = 2 * w + which
        b1, b2 = (5, 6) if which == 1 else (3, 4)
        wait("v", S["s1m"], idx)
        wait("v", S["s2m"], idx)
        if which == 2:
            wait("v", S["mixc"], 8 * w + 8)
        elif w == 0:
            wait("v", S["vev"], NJ * NEP)
        else:
            wait("v", S["out_s"], 8 * w)
        op("v", "tensor_scalar_mul", st_mean[:], pb[b1][:, 0:512], 1.0 / D,
           inc=S["vsf"])
        n1 = S["vsf"].n
        op("v", "tensor_scalar_mul", st_tmp[:], pb[b2][:, 0:512], 1.0 / D,
           inc=S["vsf"])
        wait("v", S["vsf"], n1)
        op("v", "tensor_tensor", st_var[:], st_mean[:], st_mean[:], ALU.mult,
           inc=S["vsf"])
        wait("v", S["vsf"], S["vsf"].n)
        op("v", "scalar_tensor_tensor", st_var[:], st_tmp[:], 1.0,
           st_var[:], ALU.mult, ALU.subtract, inc=S["siv2"])
        release(b1, S["siv2"], S["siv2"].n)
        release(b2, S["siv2"], S["siv2"].n)

    def ln_part2(w, which):
        idx = 2 * w + which
        op("s", "activation", c_lnsc[:, 0:1], c_m1[:, 0:1], ACT.Exp,
           bias=0.0, scale=1.0)
        wait("s", S["siv2"], idx)
        if which == 1 and w >= 1:
            wait("s", S["out_s"], 8 * w)
        op("s", "activation", st_rstd[:], st_var[:], ACT.Ln, bias=c_eps[:, 0:1],
           scale=1.0, inc=S["ssf"])
        wait("s", S["ssf"], S["ssf"].n)
        op("s", "activation", st_rstd[:], st_rstd[:], ACT.Exp, bias=0.0,
           scale=c_mh[:, 0:1], inc=S["srs"])
        wait("v", S["srs"], idx)
        op("v", "tensor_tensor", st_mr[:], st_mean[:], st_rstd[:], ALU.mult,
           inc=S["chn"])

    def ln2_tail(w):
        ln_part2(w, 2)
        ob = xw[w % 3]
        ns = []
        for c in range(NCH):
            op("v", "tensor_tensor", ob[:, c * T:(c + 1) * T],
               ob[:, c * T:(c + 1) * T], st_rstd[:], ALU.mult, inc=S["vsf"])
            ns.append(S["vsf"].n)
        wait("v", S["chn"], S["chn"].n)
        for c in range(NCH):
            wait("v", S["vsf"], ns[c])
            op("v", "tensor_tensor", ob[:, c * T:(c + 1) * T],
               ob[:, c * T:(c + 1) * T], st_mr[:], ALU.subtract,
               inc=S["out_s"])

    def stage_LN1s(w):
        ub = xw[w % 3]
        if w >= 1:
            ln2_tail(w - 1)
        ln_mms(w, 1)
        if w >= 1:
            wait("v", S["kfm"], 8 * (w - 1) + 8)
        for c in range(NCH):
            base = c * (T + 1)
            if w >= 1:
                op("v", "tensor_copy", hbuf[:, base:base + 1],
                   hbuf[:, base + T:base + T + 1])
            else:
                op("v", "memset", hbuf[:, base:base + 1], 0.0)
        ln_part1(w, 1)
        ln_part2(w, 1)
        for i in range(NCH // 2):
            c0, c1 = 2 * i, 2 * i + 1
            b0, b1 = c0 * (T + 1), c1 * (T + 1)
            op("v", "tensor_tensor", hbuf[:, b0 + 1:b0 + 1 + T],
               ub[:, c0 * T:(c0 + 1) * T], st_rstd[:], ALU.mult, inc=S["vsf"])
            nA0 = S["vsf"].n
            op("v", "tensor_tensor", hbuf[:, b1 + 1:b1 + 1 + T],
               ub[:, c1 * T:(c1 + 1) * T], st_rstd[:], ALU.mult, inc=S["vsf"])
            nA1 = S["vsf"].n
            wait("v", S["vsf"], nA0)
            op("v", "tensor_tensor", hbuf[:, b0 + 1:b0 + 1 + T],
               hbuf[:, b0 + 1:b0 + 1 + T], st_mr[:], ALU.subtract,
               inc=S["vsf"])
            nB0 = S["vsf"].n
            wait("v", S["vsf"], nA1)
            op("v", "tensor_tensor", hbuf[:, b1 + 1:b1 + 1 + T],
               hbuf[:, b1 + 1:b1 + 1 + T], st_mr[:], ALU.subtract,
               inc=S["vsf"])
            nB1 = S["vsf"].n
            wait("v", S["vsf"], nB0)
            op("v", "tensor_tensor", dd[:], hbuf[:, b0 + 1:b0 + 1 + T],
               hbuf[:, b0:b0 + T], ALU.subtract, inc=S["vsf"])
            nD0 = S["vsf"].n
            wait("v", S["vsf"], nB1)
            op("v", "tensor_tensor", dd2[:], hbuf[:, b1 + 1:b1 + 1 + T],
               hbuf[:, b1:b1 + T], ALU.subtract, inc=S["vsf"])
            nD1 = S["vsf"].n
            wait("v", S["vsf"], nD0)
            op("v", "scalar_tensor_tensor", xr[:, c0 * T:(c0 + 1) * T],
               dd[:], mur_sb[:, c0:c0 + 1], hbuf[:, b0:b0 + T],
               ALU.mult, ALU.add)
            op("v", "scalar_tensor_tensor", xk[:, c0 * T:(c0 + 1) * T],
               dd[:], muk_sb[:, c0:c0 + 1], hbuf[:, b0:b0 + T],
               ALU.mult, ALU.add, inc=S["mixc"])
            wait("v", S["vsf"], nD1)
            op("v", "scalar_tensor_tensor", xr[:, c1 * T:(c1 + 1) * T],
               dd2[:], mur_sb[:, c1:c1 + 1], hbuf[:, b1:b1 + T],
               ALU.mult, ALU.add)
            op("v", "scalar_tensor_tensor", xk[:, c1 * T:(c1 + 1) * T],
               dd2[:], muk_sb[:, c1:c1 + 1], hbuf[:, b1:b1 + T],
               ALU.mult, ALU.add, inc=S["mixc"])

    def stage_r(w):
        g = g_of[("r", w)]
        g_k = g_of[("kf", w)]
        last = (w == NWL - 1)
        wait("t", S["wld"], 32 * ((g_k if last else g) + 1))
        if w >= 1:
            wait("s", S["owe"], 8 * (w - 1) + 8)
        if not last:
            wait("t", S["mixc"], 8 * w + 8)
            for m in range(NCH):
                bk = 3 + m % 2
                acquire("t", bk)
                for k in range(NCH):
                    op("t", "matmul", pb[bk][:, 0:512],
                       wslot[g % 2][:, k * D + 128 * m: k * D + 128 * (m + 1)],
                       xr[:, k * T:(k + 1) * T],
                       start=(k == 0), stop=(k == NCH - 1),
                       inc=(S["rmm"] if k == NCH - 1 else None))
                wait("s", S["rmm"], 8 * w + m + 1)
                op("s", "activation", r_w[:, m * T:(m + 1) * T],
                   pb[bk][:, 0:512], ACT.Sigmoid, bias=0.0, scale=1.0,
                   inc=S["rsg"])
                release(bk, S["rsg"], S["rsg"].n)
            wuse_rel[g] = (S["rmm"], 8 * w + 8)
            return
        if w >= 1:
            wait("v", S["owm"], 8 * (w - 1) + 8)
        rbanks = [0, 1, 7, 2]
        kbanks = [3, 4, 5, 6]
        for half in range(2):
            ms = [0, 1, 2, 3] if half == 0 else [4, 5, 6, 7]
            for mi in range(4):
                acquire("t", rbanks[mi])
                acquire("t", kbanks[mi])
            for k in range(NCH):
                if half == 0:
                    wait("t", S["mixc"], 8 * w + k + 1)
                for mi, m in enumerate(ms):
                    op("t", "matmul", pb[rbanks[mi]][:, 0:512],
                       wslot[g % 2][:, k * D + 128 * m: k * D + 128 * (m + 1)],
                       xr[:, k * T:(k + 1) * T],
                       start=(k == 0), stop=(k == NCH - 1),
                       inc=(S["rmm"] if k == NCH - 1 else None))
                    op("t", "matmul", pb[kbanks[mi]][:, 0:512],
                       wslot[g_k % 2][:, k * D + 128 * m: k * D + 128 * (m + 1)],
                       xk[:, k * T:(k + 1) * T],
                       start=(k == 0), stop=(k == NCH - 1),
                       inc=(S["kfm"] if k == NCH - 1 else None))
            for mi, m in enumerate(ms):
                wait("s", S["rmm"], 8 * w + 4 * half + mi + 1)
                op("s", "activation", r_w[:, m * T:(m + 1) * T],
                   pb[rbanks[mi]][:, 0:512], ACT.Sigmoid, bias=0.0, scale=1.0,
                   inc=S["rsg"])
                release(rbanks[mi], S["rsg"], S["rsg"].n)
                wait("s", S["kfm"], 8 * w + 4 * half + mi + 1)
                op("s", "activation", ka[:, m * T:(m + 1) * T],
                   pb[kbanks[mi]][:, 0:512], ACT.Relu, bias=0.0, scale=1.0,
                   inc=S["kfr"])
                release(kbanks[mi], S["kfr"], S["kfr"].n)
                wait("v", S["kfr"], 8 * w + 4 * half + mi + 1)
                op("v", "tensor_tensor", ka[:, m * T:(m + 1) * T],
                   ka[:, m * T:(m + 1) * T], ka[:, m * T:(m + 1) * T],
                   ALU.mult, inc=S["ka_s"])
        wuse_rel[g] = (S["rmm"], 8 * w + 8)
        wuse_rel[g_k] = (S["kfm"], 8 * w + 8)

    def stage_kf(w):
        if w == NWL - 1:
            return
        g = g_of[("kf", w)]
        wait("t", S["wld"], 32 * (g + 1))
        if w >= 1:
            wait("v", S["owm"], 8 * (w - 1) + 8)
        for m in range(NCH):
            bk = 5 + m % 2
            acquire("t", bk)
            for k in range(NCH):
                op("t", "matmul", pb[bk][:, 0:512],
                   wslot[g % 2][:, k * D + 128 * m: k * D + 128 * (m + 1)],
                   xk[:, k * T:(k + 1) * T],
                   start=(k == 0), stop=(k == NCH - 1),
                   inc=(S["kfm"] if k == NCH - 1 else None))
            wait("s", S["kfm"], 8 * w + m + 1)
            op("s", "activation", ka[:, m * T:(m + 1) * T],
               pb[bk][:, 0:512], ACT.Relu, bias=0.0, scale=1.0,
               inc=S["kfr"])
            release(bk, S["kfr"], S["kfr"].n)
            wait("v", S["kfr"], 8 * w + m + 1)
            op("v", "tensor_tensor", ka[:, m * T:(m + 1) * T],
               ka[:, m * T:(m + 1) * T], ka[:, m * T:(m + 1) * T],
               ALU.mult, inc=S["ka_s"])
        wuse_rel[g] = (S["kfm"], 8 * w + 8)

    def stage_ow(w):
        g = g_of[("ow", w)]
        ub = xw[w % 3]
        wait("t", S["wld"], 32 * (g + 1))
        wait("t", S["ka_s"], 8 * w + 8)
        wait("v", S["rsg"], 8 * w + 8)
        for m in range(NCH):
            bk = 3 + m % 2
            base = m * (T + 1)
            acquire("t", bk)
            for k in range(NCH):
                op("t", "matmul", pb[bk][:, 0:512],
                   wslot[g % 2][:, k * D + 128 * m: k * D + 128 * (m + 1)],
                   ka[:, k * T:(k + 1) * T],
                   start=(k == 0), stop=(k == NCH - 1),
                   inc=(S["owm"] if k == NCH - 1 else None))
            wait("v", S["owm"], 8 * w + m + 1)
            if m >= 1:
                wait("v", S["owe"], 8 * w + m)
            op("v", "tensor_tensor", dd[:], pb[bk][:, 0:512],
               r_w[:, m * T:(m + 1) * T], ALU.mult, inc=S["vsf"])
            release(bk, S["vsf"], S["vsf"].n)
            wait("v", S["vsf"], S["vsf"].n)
            op("v", "tensor_tensor", ub[:, m * T:(m + 1) * T], dd[:],
               hbuf[:, base + 1:base + 1 + T], ALU.add, inc=S["owe"])
        wuse_rel[g] = (S["owm"], 8 * w + 8)
        if w + 2 < NWL:
            emit_convert(w + 2)

    def stage_LN2s(w):
        ln_mms(w, 2)
        ln_part1(w, 2)
        ob = xw[w % 3]
        if w == NWL - 1:
            ln2_tail(w)
        for c in range(NCH):
            wait("y", S["out_s"], 8 * w + c + 1)
            dma("y", outT[c * 128:(c + 1) * 128, T * w:T * (w + 1)],
                ob[:, c * T:(c + 1) * T], S["sto"])
        if w + 2 < NWL:
            wait("y", S["sto"], 128 * (w + 1))
            emit_xw_load(w + 2)
            xw_loaded[0] += 1

    emitters = {"Q": stage_Q, "S": stage_S, "y": stage_y, "Wo": stage_Wo,
                "LN1s": stage_LN1s, "r": stage_r, "kf": stage_kf,
                "ow": stage_ow, "LN2s": stage_LN2s}

    cons_idx = [0]

    for (st_name, w) in stages:
        emitters[st_name](w)
        if st_name in tag_of:
            cons_idx[0] += 1
            nl = next_load[0]
            if nl < len(cons_order) and nl == cons_idx[0] + 1:
                tag, ww = cons_order[nl][0], cons_order[nl][1]
                emit_wload(tag_of[tag])
                next_load[0] += 1

    if NWL == 1 and dbg_store:
        wait("g", S["out_s"], 8)
        dma("g", outT[512:640, :], xr[:], S["sto"])
        dma("g", outT[640:768, :], xk[:], S["sto"])
        dma("g", outT[768:896, :], ka[:], S["sto"])
        dma("g", outT[896:1024, :], r_w[:], S["sto"])
    if NWL == 0:
        if do_a2:
            wait("g", S["cev"], 8 * NEP)
            dma("g", outT[0:128, :], ctx_bf[:, 0:N], S["sto"])
            dma("g", outT[128:256, :], ctx_bf[:, N:2 * N], S["sto"])
            dma("g", outT[256:384, 0:D], sd_acc[:], S["sto"])
            dma("g", outT[384:512, 0:8], sinv[:], S["sto"])
            dma("g", outT[512:640, 0:D], sd_hi[:], S["sto"])
            dma("g", outT[640:768, 0:D], sd_lo[:], S["sto"])
            dma("g", outT[768:896, 0:8], srec[:, 0:8], S["sto"])
        else:
            wait("g", S["kev"], NJ)
            wait("g", S["siv"], 1)
            dma("g", outT[0:128, :], kexp[:, 0:N], S["sto"])

    wait("y", S["sto"], S["sto"].n)

    with es:
        with nc.Block() as block:
            @block.tensor
            def _(e):
                for f in TT:
                    f(e)

            @block.vector
            def _(e):
                for f in VV:
                    f(e)

            @block.scalar
            def _(e):
                for f in SS:
                    f(e)

            @block.sync
            def _(e):
                for f in SY:
                    f(e)

            @block.gpsimd
            def _(e):
                for f in GP:
                    f(e)
    return nc


_NC_CACHE = None


def kernel(L0_attn_x, L0_attn_Wq, L0_attn_Wk, L0_attn_Wv, L0_attn_Wo,
           L0_attn_bo, L0_ffn_W_r, L0_ffn_W_k, L0_ffn_W_v, L0_ffn_mu_r,
           L0_ffn_mu_k):
    global _NC_CACHE
    x = np.asarray(L0_attn_x, np.float32)
    B = x.shape[0]
    if _NC_CACHE is None:
        _NC_CACHE = build_nc()
    nc = _NC_CACHE
    base = {
        "wq": np.ascontiguousarray(np.asarray(L0_attn_Wq, np.float32)),
        "wk": np.ascontiguousarray(np.asarray(L0_attn_Wk, np.float32)),
        "wv": np.ascontiguousarray(np.asarray(L0_attn_Wv, np.float32)),
        "wo": np.ascontiguousarray(np.asarray(L0_attn_Wo, np.float32)),
        "wrT": np.ascontiguousarray(np.asarray(L0_ffn_W_r, np.float32).T),
        "wkT": np.ascontiguousarray(np.asarray(L0_ffn_W_k, np.float32).T),
        "wvT": np.ascontiguousarray(np.asarray(L0_ffn_W_v, np.float32).T),
        "bo": np.ascontiguousarray(np.asarray(L0_attn_bo, np.float32)),
        "mur": np.ascontiguousarray(np.asarray(L0_ffn_mu_r, np.float32)),
        "muk": np.ascontiguousarray(np.asarray(L0_ffn_mu_k, np.float32)),
    }
    in_maps = []
    for b in range(B):
        m = dict(base)
        m["xT"] = np.ascontiguousarray(x[b].T)
        in_maps.append(m)
    res = run_bass_kernel_spmd(nc, in_maps, core_ids=list(range(B)))
    out = np.empty_like(x)
    for b in range(B):
        out[b] = res.results[b]["outT"].T
    return out


# version 48 (final)
# speedup vs baseline: 1.0102x; 1.0102x over previous
import numpy as np
import concourse.bass as bass
import concourse.mybir as mybir
from concourse.bass_utils import run_bass_kernel_spmd
from contextlib import ExitStack

F32 = mybir.dt.float32
F32R = mybir.dt.float32r
BF16 = mybir.dt.bfloat16
ALU = mybir.AluOpType
ACT = mybir.ActivationFunctionType

D = 1024
N = 4096
NCH = 8
NJ = 32
T = 512
NW = N // T
EW = 256
NEP = D // EW
CTX_SCALE = 0.125 / 1024.0
EPS = 1e-5


class Sem:
    def __init__(self, es, nc, name):
        self.h = es.enter_context(nc.semaphore(name))
        self.n = 0


def build_nc(do_a2=True, b_windows=NW, dbg_store=None):
    nc = bass.Bass("TRN2", target_bir_lowering=False)

    xT = nc.dram_tensor("xT", [D, N], F32, kind="ExternalInput").ap()
    wq_h = nc.dram_tensor("wq", [D, D], F32, kind="ExternalInput").ap()
    wk_h = nc.dram_tensor("wk", [D, D], F32, kind="ExternalInput").ap()
    wv_h = nc.dram_tensor("wv", [D, D], F32, kind="ExternalInput").ap()
    wo_h = nc.dram_tensor("wo", [D, D], F32, kind="ExternalInput").ap()
    wrT_h = nc.dram_tensor("wrT", [D, D], F32, kind="ExternalInput").ap()
    wkT_h = nc.dram_tensor("wkT", [D, D], F32, kind="ExternalInput").ap()
    wvT_h = nc.dram_tensor("wvT", [D, D], F32, kind="ExternalInput").ap()
    bo_h = nc.dram_tensor("bo", [D], F32, kind="ExternalInput").ap()
    mur_h = nc.dram_tensor("mur", [D], F32, kind="ExternalInput").ap()
    muk_h = nc.dram_tensor("muk", [D], F32, kind="ExternalInput").ap()
    outT = nc.dram_tensor("outT", [D, N], F32, kind="ExternalOutput").ap()
    wbf_scr = {t: nc.dram_tensor(f"wscr_{t}", [D, D], BF16, kind="Internal").ap()
               for t in ("wq", "wo", "wr", "wk", "wv")}

    def _szb(shape, dtype):
        n = 1
        for s in shape[1:]:
            n *= s
        n *= mybir.dt.size(dtype)
        return (n + 31) & ~31

    spec_p = {
        "ctx_bf": ([128, NCH * D], BF16),
        "ones_bf": ([128, 128], BF16),
        "ones_col": ([128, 1], BF16),
        "sinv": ([128, 8], F32),
        "bo_sb": ([128, 8], F32),
        "c_m1": ([128, 1], F32),
        "c_mh": ([128, 1], F32),
        "c_eps": ([128, 1], F32),
        "c_lnsc": ([128, 1], F32),
        "mur_sb": ([128, 8], F32),
        "muk_sb": ([128, 8], F32),
    }
    spec_a = {
        "x_bf": ([128, NCH * N], BF16),
        "kexp": ([128, NJ * D], BF16),
        "v_sb": ([128, NJ * EW], BF16),
        "wk_bf": ([128, NCH * D], BF16),
        "wv_bf": ([128, NCH * D], BF16),
        "sd_acc": ([128, D], F32),
        "sd_acc2": ([128, D], F32),
        "sd_hi": ([128, D], BF16),
        "sd_lo": ([128, D], BF16),
    }
    spec_b = {
        "wslot0": ([128, NCH * D], BF16),
        "wslot1": ([128, NCH * D], BF16),
        "xw0": ([128, NCH * T], F32),
        "xw1": ([128, NCH * T], F32),
        "xw2": ([128, NCH * T], F32),
        "qexp": ([128, NCH * T], BF16),
        "y_bf": ([128, NCH * T], BF16),
        "u1_bf": ([128, NCH * T], BF16),
        "usq": ([128, NCH * T], BF16),
        "hbuf": ([128, NCH * (T + 1)], F32),
        "r_w": ([128, NCH * T], BF16),
        "xr": ([128, NCH * T], BF16),
        "xk": ([128, NCH * T], BF16),
        "ka": ([128, NCH * T], BF16),
        "dd": ([128, T], F32),
        "t3": ([128, T], F32),
        "srec": ([128, T], F32),
        "st_mean": ([128, T], F32),
        "st_var": ([128, T], F32),
        "st_rstd": ([128, T], F32),
        "st_mr": ([128, T], F32),
        "xbfw": ([128, NCH * T], BF16),
        "xbfw1": ([128, NCH * T], BF16),
    }
    p_bytes = sum(_szb(*v) for v in spec_p.values())
    a_bytes = sum(_szb(*v) for v in spec_a.values())
    b_bytes = sum(_szb(*v) for v in spec_b.values())
    total = p_bytes + max(a_bytes, b_bytes)
    base = nc.alloc_sbuf_tensor("arena", [128, total], mybir.dt.uint8)
    addr0 = nc.lookup_mloc(base).addr
    H = {}
    off = addr0
    for nm, (shape, dt_) in spec_p.items():
        H[nm] = nc.alloc_sbuf_tensor_at(nm, shape, dt_, offset=off)
        off += _szb(shape, dt_)
    ov0 = off
    for spec in (spec_a, spec_b):
        off = ov0
        for nm, (shape, dt_) in spec.items():
            H[nm] = nc.alloc_sbuf_tensor_at(nm, shape, dt_, offset=off)
            off += _szb(shape, dt_)
    ctx_bf = H["ctx_bf"]; ones_bf = H["ones_bf"]; ones_col = H["ones_col"]
    sinv = H["sinv"]; bo_sb = H["bo_sb"]; mur_sb = H["mur_sb"]; muk_sb = H["muk_sb"]
    c_m1 = H["c_m1"]; c_mh = H["c_mh"]; c_eps = H["c_eps"]; c_lnsc = H["c_lnsc"]
    x_bf = H["x_bf"]; kexp = H["kexp"]; v_sb = H["v_sb"]
    wk_bf = H["wk_bf"]; wv_bf = H["wv_bf"]; sd_acc = H["sd_acc"]
    sd_acc2 = H["sd_acc2"]; sd_hi = H["sd_hi"]; sd_lo = H["sd_lo"]
    wslot = [H["wslot0"], H["wslot1"]]
    xw = [H["xw0"], H["xw1"], H["xw2"]]
    xbfw = [H["xbfw"], H["xbfw1"]]; qexp = H["qexp"]; y_bf = H["y_bf"]
    u1_bf = H["u1_bf"]; usq = H["usq"]; hbuf = H["hbuf"]; r_w = H["r_w"]
    xr = H["xr"]; xk = H["xk"]; ka = H["ka"]
    dd = H["dd"]; t2 = H["dd"]; t3 = H["t3"]; dd2 = H["t3"]; srec = H["srec"]
    st_mean = H["st_mean"]; st_tmp = H["t3"]; st_var = H["st_var"]
    st_rstd = H["st_rstd"]; st_mr = H["st_mr"]

    pb = [nc.alloc_psum_tensor(f"pb{i}", [128, 512], F32) for i in range(8)]

    TT, VV, SS, SY, GP = [], [], [], [], []
    streams = {"t": TT, "v": VV, "s": SS, "y": SY, "g": GP}

    def wait(st, sem, v):
        if v > 0:
            streams[st].append(lambda e, s=sem.h, vv=v: e.wait_ge(s, vv))

    def op(st, meth, *args, inc=None, inck=1, **kw):
        if inc is not None:
            inc.n += inck
            tgt = (inc.h, inck)
        else:
            tgt = None

        def f(e, m=meth, a=args, k=kw, t=tgt):
            inst = getattr(e, m)(*a, **k)
            if t is not None:
                inst.then_inc(*t)
        streams[st].append(f)

    es = ExitStack()
    sem_names = [
        "ldm", "xbf", "wa", "wav", "kmm", "kev", "sda", "sdm", "siv", "vmm",
        "vev", "cmm", "cev", "wld", "sxw", "sxb", "qmm", "qev", "smm", "sre",
        "ymm", "yev", "womm", "su1", "usq_s", "u1b", "s1m", "s2m", "siv2",
        "srs", "chn", "mix", "rmm", "rsg", "kfm", "kfr", "ka_s", "owm", "owe",
        "out_s", "sto", "ini", "sdh", "mixg", "mixc",
        "wc_wq", "wc_wo", "wc_wr", "wc_wk", "wc_wv", "vsf", "ssf",
    ]
    S = {n: Sem(es, nc, n) for n in sem_names}

    bank_rel = [None] * 8

    def acquire(st, *banks):
        for b in banks:
            if bank_rel[b] is not None:
                sem, v = bank_rel[b]
                wait(st, sem, v)

    def release(b, sem, v):
        bank_rel[b] = (sem, v)

    def dma(st, out_ap, in_ap, inc, **kw):
        op(st, "dma_start", inc=inc, inck=16, out=out_ap, in_=in_ap, **kw)

    dma("y", bo_sb[:], bo_h.rearrange("(c p) -> p c", p=128), S["ldm"],
        allow_slow_non_contiguous=True)
    dma("y", mur_sb[:], mur_h.rearrange("(c p) -> p c", p=128), S["ldm"],
        allow_slow_non_contiguous=True)
    dma("y", muk_sb[:], muk_h.rearrange("(c p) -> p c", p=128), S["ldm"],
        allow_slow_non_contiguous=True)

    op("v", "memset", ones_bf[:], 1.0, inc=S["ini"])
    op("v", "memset", ones_col[:], 1.0, inc=S["ini"])
    op("v", "memset", c_m1[:], -1.0)
    op("v", "memset", c_mh[:], -0.5)
    op("v", "memset", c_eps[:], EPS)
    op("v", "memset", c_lnsc[:], float(np.log(CTX_SCALE)), inc=S["ini"])

    def wchunk_load(dst, src_h, sem):
        for c in range(NCH):
            dma("g", dst[:, c * D:(c + 1) * D],
                src_h[c * 128:(c + 1) * 128, :], sem)

    wchunk_load(wk_bf, wk_h, S["wa"])
    for s_i in range(8):
        wait("g", S["xbf"], 128 * s_i)
        for c in range(NCH):
            dma("g", x_bf[:, c * N + 512 * s_i: c * N + 512 * (s_i + 1)],
                xT[c * 128:(c + 1) * 128, 512 * s_i:512 * (s_i + 1)], S["xbf"])
        if s_i == 0:
            wchunk_load(wv_bf, wv_h, S["wav"])
    wsrc_f32 = {"wq": wq_h, "wo": wo_h, "wr": wrT_h, "wk": wkT_h, "wv": wvT_h}
    for t in ("wq", "wo", "wr", "wk", "wv"):
        dma("g", wbf_scr[t][:, :], wsrc_f32[t][:, :], S["wc_" + t])

    for j in range(NJ):
        p = j % 2
        b0, b1 = 2 * p, 2 * p + 1
        s_i = j // 4
        wait("t", S["xbf"], 128 * (s_i + 1))
        if j == 0:
            wait("t", S["wa"], 128)
        acquire("t", b0, b1)
        for i in range(NCH):
            st = (i == 0)
            sp = (i == NCH - 1)
            xtile = x_bf[:, i * N + 128 * j: i * N + 128 * (j + 1)]
            op("t", "matmul", pb[b0][:, 0:512], xtile,
               wk_bf[:, i * D + 0: i * D + 512], start=st, stop=sp)
            op("t", "matmul", pb[b1][:, 0:512], xtile,
               wk_bf[:, i * D + 512: i * D + 1024], start=st, stop=sp,
               inc=(S["kmm"] if (i == NCH - 1) else None))
        wait("s", S["kmm"], j + 1)
        op("s", "activation", kexp[:, j * D + 0: j * D + 512], pb[b0][:, 0:512],
           ACT.Exp, bias=0.0, scale=1.0)
        op("s", "activation", kexp[:, j * D + 512: j * D + 1024], pb[b1][:, 0:512],
           ACT.Exp, bias=0.0, scale=1.0, inc=S["kev"])
        release(b0, S["kev"], S["kev"].n)
        release(b1, S["kev"], S["kev"].n)
        wait("v", S["kev"], j + 1)
        acc = sd_acc if j % 2 == 0 else sd_acc2
        if j < 2:
            op("v", "tensor_copy", acc[:], kexp[:, j * D:(j + 1) * D],
               inc=S["sda"])
        else:
            wait("v", S["sda"], j - 1)
            op("v", "tensor_add", acc[:], acc[:],
               kexp[:, j * D:(j + 1) * D], inc=S["sda"])
    wait("v", S["sda"], NJ)
    op("v", "tensor_add", sd_acc[:], sd_acc[:], sd_acc2[:], inc=S["sda"])

    wait("s", S["sda"], NJ + 1)
    op("s", "activation", sd_hi[:], sd_acc[:], ACT.Copy, bias=0.0, scale=1.0,
       inc=S["sdh"])
    wait("v", S["sdh"], 1)
    op("v", "tensor_tensor", sd_lo[:], sd_acc[:], sd_hi[:], ALU.subtract,
       inc=S["sda"])

    for t_i in range(NEP if do_a2 else 0):
        e0 = EW * t_i
        if t_i == 0:
            wait("t", S["wav"], 128)
        for j in range(NJ):
            bk = 6 + (j % 2)
            acquire("t", bk)
            for i in range(NCH):
                st = (i == 0)
                sp = (i == NCH - 1)
                xtile = x_bf[:, i * N + 128 * j: i * N + 128 * (j + 1)]
                op("t", "matmul", pb[bk][:, 0:EW], xtile,
                   wv_bf[:, i * D + e0: i * D + e0 + EW], start=st, stop=sp,
                   inc=(S["vmm"] if i == NCH - 1 else None))
            wait("v", S["vmm"], NJ * t_i + j + 1)
            op("v", "tensor_copy", v_sb[:, j * EW:(j + 1) * EW], pb[bk][:, 0:EW],
               inc=S["vev"])
            release(bk, S["vev"], S["vev"].n)
        if t_i == 0:
            wait("t", S["sda"], NJ + 2)
            wait("t", S["ini"], 2)
            acquire("t", 4)
            for c in range(NCH):
                op("t", "matmul", pb[4][:, c:c + 1],
                   sd_hi[:, c * 128:(c + 1) * 128],
                   ones_col[:], start=True, stop=False)
                op("t", "matmul", pb[4][:, c:c + 1],
                   sd_lo[:, c * 128:(c + 1) * 128],
                   ones_col[:], start=False, stop=True,
                   inc=(S["sdm"] if c == NCH - 1 else None))
            wait("s", S["sdm"], 1)
            wait("s", S["ini"], 3)
            op("s", "activation", sinv[:], pb[4][:, 0:8], ACT.Ln, bias=0.0,
               scale=1.0, inc=S["ssf"])
            wait("s", S["ssf"], S["ssf"].n)
            op("s", "activation", sinv[:], sinv[:], ACT.Exp,
               bias=c_lnsc[:, 0:1], scale=c_m1[:, 0:1], inc=S["siv"])
            release(4, S["siv"], 1)
        wait("t", S["vev"], NJ * (t_i + 1))
        acquire("t", *range(8))
        for j in range(NJ):
            for c in range(NCH):
                op("t", "matmul", pb[c][:, 0:EW],
                   kexp[:, j * D + c * 128: j * D + (c + 1) * 128],
                   v_sb[:, j * EW:(j + 1) * EW],
                   start=(j == 0), stop=(j == NJ - 1),
                   inc=(S["cmm"] if (j == NJ - 1 and c == NCH - 1) else None))
        wait("s", S["cmm"], t_i + 1)
        if t_i == 0:
            wait("s", S["siv"], 1)
        for c in [6, 7, 0, 1, 2, 3, 4, 5]:
            op("s", "activation", ctx_bf[:, c * D + e0: c * D + e0 + EW],
               pb[c][:, 0:EW], ACT.Copy, bias=0.0, scale=sinv[:, c:c + 1],
               inc=S["cev"])
            release(c, S["cev"], S["cev"].n)

    wsrc = {"wq": wq_h, "wo": wo_h, "wr": wrT_h, "wk": wkT_h, "wv": wvT_h}
    wuse_rel = {}
    g_loaded = [0]
    g_pending = []
    wc_waited = set()

    def emit_wload(tag):
        g = g_loaded[0]
        if g >= 2:
            rel = wuse_rel[g - 2]
            wait("g", rel[0], rel[1])
        if g == 0:
            wait("g", S["vmm"], NJ * NEP)
        if tag not in wc_waited:
            wait("g", S["wc_" + tag], 16)
            wc_waited.add(tag)
        wait("g", S["wld"], 32 * g)
        src = wbf_scr[tag].rearrange("(k p) m -> p k m", p=128)
        dst = wslot[g % 2][:].rearrange("p (k m) -> p k m", k=NCH)
        for h in range(2):
            dma("g", dst[:, 4 * h:4 * (h + 1), :], src[:, 4 * h:4 * (h + 1), :],
                S["wld"])
        g_loaded[0] += 1
        return g

    def emit_xw_load(w):
        if w == 0:
            wait("y", S["vmm"], NJ * NEP)
        wait("y", S["sxw"], 128 * w)
        for c in range(NCH):
            dma("y", xw[w % 3][:, c * T:(c + 1) * T],
                xT[c * 128:(c + 1) * 128, T * w:T * (w + 1)], S["sxw"])

    NWL = b_windows
    if NWL > 0:
        stages = [("Q", 0), ("S", 0), ("y", 0), ("Wo", 0), ("LN1s", 0)]
        if NWL > 1:
            stages += [("Q", 1), ("S", 1), ("y", 1)]
        for w in range(NWL):
            stages += [("r", w), ("kf", w), ("ow", w), ("LN2s", w)]
            if w + 1 < NWL:
                stages += [("Wo", w + 1), ("LN1s", w + 1)]
            if w + 2 < NWL:
                stages += [("Q", w + 2), ("S", w + 2), ("y", w + 2)]
    else:
        stages = []

    cons_order = [(st, w) for (st, w) in stages if st in ("Q", "Wo", "r", "kf", "ow")]
    tag_of = {"Q": "wq", "Wo": "wo", "r": "wr", "kf": "wk", "ow": "wv"}
    g_of = {}
    for gi, (st, w) in enumerate(cons_order):
        g_of[(st, w)] = gi

    next_load = [2]
    xw_loaded = [2]
    if NWL > 0:
        emit_wload("wq")
        emit_wload("wo")
        emit_xw_load(0)
        if NWL > 1:
            emit_xw_load(1)

    def emit_convert(w):
        if w <= 1:
            wait("g", S["vmm"], NJ * NEP)
        else:
            wait("g", S["qmm"], 8 * (w - 2) + 8)
        wait("g", S["sxb"], 16 * w)
        dma("g", xbfw[w % 2][:].rearrange("p (c n) -> p c n", c=NCH),
            xT.rearrange("(c p) n -> p c n", p=128)[:, :, T * w:T * (w + 1)],
            S["sxb"])

    if NWL > 0:
        emit_convert(0)
        if NWL > 1:
            emit_convert(1)

    def stage_Q(w):
        g = g_of[("Q", w)]
        wait("t", S["wld"], 32 * (g + 1))
        wait("t", S["sxb"], 16 * (w + 1))
        if w >= 1:
            wait("s", S["ymm"], 8 * (w - 1) + 8)
        qbanks = [0, 1, 7, 2, 0, 1, 7, 2]
        for m in range(NCH):
            bk = qbanks[m]
            acquire("t", bk)
            for k in range(NCH):
                op("t", "matmul", pb[bk][:, 0:512],
                   wslot[g % 2][:, k * D + 128 * m: k * D + 128 * (m + 1)],
                   xbfw[w % 2][:, k * T:(k + 1) * T],
                   start=(k == 0), stop=(k == NCH - 1),
                   inc=(S["qmm"] if k == NCH - 1 else None))
            wait("s", S["qmm"], 8 * w + m + 1)
            op("s", "activation", qexp[:, m * T:(m + 1) * T], pb[bk][:, 0:512],
               ACT.Exp, bias=0.0, scale=1.0, inc=S["qev"])
            release(bk, S["qev"], S["qev"].n)
        wuse_rel[g] = (S["qmm"], 8 * w + 8)

    def stage_S(w):
        wait("t", S["qev"], 8 * w + 8)
        acquire("t", 2)
        for d_i in range(NCH):
            op("t", "matmul", pb[2][:, 0:512], ones_bf[:],
               qexp[:, d_i * T:(d_i + 1) * T],
               start=(d_i == 0), stop=(d_i == NCH - 1),
               inc=(S["smm"] if d_i == NCH - 1 else None))
        wait("s", S["smm"], w + 1)
        if w >= 1:
            wait("s", S["yev"], 8 * (w - 1) + 8)
        op("s", "activation", srec[:], pb[2][:, 0:512], ACT.Ln, bias=0.0,
           scale=1.0, inc=S["ssf"])
        wait("s", S["ssf"], S["ssf"].n)
        op("s", "activation", srec[:], srec[:], ACT.Exp, bias=0.0,
           scale=c_m1[:, 0:1], inc=S["sre"])
        release(2, S["sre"], S["sre"].n)

    def stage_y(w):
        if w == 0:
            wait("t", S["cev"], 8 * NEP)
        wait("v", S["sre"], w + 1)
        if w >= 1:
            wait("v", S["womm"], 8 * (w - 1) + 8)
        ybanks = [3, 4, 5, 6, 0, 1, 7, 2]
        for e in range(NCH):
            bk = ybanks[e]
            acquire("t", bk)
            for d_i in range(NCH):
                op("t", "matmul", pb[bk][:, 0:512],
                   ctx_bf[:, d_i * D + 128 * e: d_i * D + 128 * (e + 1)],
                   qexp[:, d_i * T:(d_i + 1) * T],
                   start=(d_i == 0), stop=(d_i == NCH - 1),
                   inc=(S["ymm"] if d_i == NCH - 1 else None))
            wait("v", S["ymm"], 8 * w + e + 1)
            op("v", "tensor_tensor", y_bf[:, e * T:(e + 1) * T],
               pb[bk][:, 0:512], srec[:], ALU.mult, inc=S["yev"])
            release(bk, S["yev"], S["yev"].n)

    def stage_Wo(w):
        g = g_of[("Wo", w)]
        ub = xw[w % 3]
        wait("t", S["wld"], 32 * (g + 1))
        wait("t", S["yev"], 8 * w + 8)
        if w == 0:
            wait("v", S["ldm"], 48)
        wait("v", S["sxw"], 128 * (w + 1))
        wobanks = [5, 6, 3, 4, 5, 6, 3, 4]
        for o in range(NCH):
            bk = wobanks[o]
            acquire("t", bk)
            for e in range(NCH):
                op("t", "matmul", pb[bk][:, 0:512],
                   wslot[g % 2][:, e * D + 128 * o: e * D + 128 * (o + 1)],
                   y_bf[:, e * T:(e + 1) * T],
                   start=(e == 0), stop=(e == NCH - 1),
                   inc=(S["womm"] if e == NCH - 1 else None))
            wait("v", S["womm"], 8 * w + o + 1)
            op("v", "scalar_tensor_tensor", ub[:, o * T:(o + 1) * T],
               pb[bk][:, 0:512], bo_sb[:, o:o + 1],
               ub[:, o * T:(o + 1) * T], ALU.add, ALU.add, inc=S["su1"])
            release(bk, S["su1"], S["su1"].n)
        wuse_rel[g] = (S["womm"], 8 * w + 8)

    def ln_mms(w, which):
        idx = 2 * w + which
        b1, b2 = (5, 6) if which == 1 else (3, 4)
        ub = xw[w % 3]
        src = S["su1"] if which == 1 else S["owe"]
        if idx >= 2:
            wait("s", S["s2m"], idx - 1)
            wait("s", S["s1m"], idx - 1)
        acquire("t", b1)
        acquire("t", b2)
        for d_i in range(NCH):
            wait("s", src, 8 * w + d_i + 1)
            op("s", "activation", usq[:, d_i * T:(d_i + 1) * T],
               ub[:, d_i * T:(d_i + 1) * T], ACT.Square, bias=0.0, scale=1.0,
               inc=S["usq_s"])
            op("s", "activation", u1_bf[:, d_i * T:(d_i + 1) * T],
               ub[:, d_i * T:(d_i + 1) * T], ACT.Copy, bias=0.0, scale=1.0,
               inc=S["u1b"])
            wait("t", S["u1b"], 8 * (idx - 1) + d_i + 1)
            op("t", "matmul", pb[b1][:, 0:512], ones_bf[:],
               u1_bf[:, d_i * T:(d_i + 1) * T],
               start=(d_i == 0), stop=(d_i == NCH - 1),
               inc=(S["s1m"] if d_i == NCH - 1 else None))
            wait("t", S["usq_s"], 8 * (idx - 1) + d_i + 1)
            op("t", "matmul", pb[b2][:, 0:512], ones_bf[:],
               usq[:, d_i * T:(d_i + 1) * T],
               start=(d_i == 0), stop=(d_i == NCH - 1),
               inc=(S["s2m"] if d_i == NCH - 1 else None))

    def ln_part1(w, which):
        idx = 2 * w + which
        b1, b2 = (5, 6) if which == 1 else (3, 4)
        wait("v", S["s1m"], idx)
        wait("v", S["s2m"], idx)
        if which == 2:
            wait("v", S["mixc"], 8 * w + 8)
        elif w == 0:
            wait("v", S["vev"], NJ * NEP)
        else:
            wait("v", S["out_s"], 8 * w)
        op("v", "tensor_scalar_mul", st_mean[:], pb[b1][:, 0:512], 1.0 / D,
           inc=S["vsf"])
        n1 = S["vsf"].n
        op("v", "tensor_scalar_mul", st_tmp[:], pb[b2][:, 0:512], 1.0 / D,
           inc=S["vsf"])
        wait("v", S["vsf"], n1)
        op("v", "tensor_tensor", st_var[:], st_mean[:], st_mean[:], ALU.mult,
           inc=S["vsf"])
        wait("v", S["vsf"], S["vsf"].n)
        op("v", "scalar_tensor_tensor", st_var[:], st_tmp[:], 1.0,
           st_var[:], ALU.mult, ALU.subtract, inc=S["siv2"])
        release(b1, S["siv2"], S["siv2"].n)
        release(b2, S["siv2"], S["siv2"].n)

    def ln_part2(w, which):
        idx = 2 * w + which
        wait("s", S["siv2"], idx)
        if which == 1 and w >= 1:
            wait("s", S["out_s"], 8 * w)
        op("s", "activation", st_rstd[:], st_var[:], ACT.Ln, bias=c_eps[:, 0:1],
           scale=1.0, inc=S["ssf"])
        wait("s", S["ssf"], S["ssf"].n)
        op("s", "activation", st_rstd[:], st_rstd[:], ACT.Exp, bias=0.0,
           scale=c_mh[:, 0:1], inc=S["srs"])
        wait("v", S["srs"], idx)
        op("v", "tensor_tensor", st_mr[:], st_mean[:], st_rstd[:], ALU.mult,
           inc=S["chn"])

    def ln2_tail(w):
        ln_part2(w, 2)
        ob = xw[w % 3]
        ns = []
        for c in range(NCH):
            op("v", "tensor_tensor", ob[:, c * T:(c + 1) * T],
               ob[:, c * T:(c + 1) * T], st_rstd[:], ALU.mult, inc=S["vsf"])
            ns.append(S["vsf"].n)
        wait("v", S["chn"], S["chn"].n)
        for c in range(NCH):
            wait("v", S["vsf"], ns[c])
            op("v", "tensor_tensor", ob[:, c * T:(c + 1) * T],
               ob[:, c * T:(c + 1) * T], st_mr[:], ALU.subtract,
               inc=S["out_s"])

    def stage_LN1s(w):
        ub = xw[w % 3]
        if w >= 1:
            ln2_tail(w - 1)
        ln_mms(w, 1)
        if w >= 1:
            wait("v", S["kfm"], 8 * (w - 1) + 8)
        for c in range(NCH):
            base = c * (T + 1)
            if w >= 1:
                op("v", "tensor_copy", hbuf[:, base:base + 1],
                   hbuf[:, base + T:base + T + 1])
            else:
                op("v", "memset", hbuf[:, base:base + 1], 0.0)
        ln_part1(w, 1)
        ln_part2(w, 1)
        for i in range(NCH // 2):
            c0, c1 = 2 * i, 2 * i + 1
            b0, b1 = c0 * (T + 1), c1 * (T + 1)
            op("v", "tensor_tensor", hbuf[:, b0 + 1:b0 + 1 + T],
               ub[:, c0 * T:(c0 + 1) * T], st_rstd[:], ALU.mult, inc=S["vsf"])
            nA0 = S["vsf"].n
            op("v", "tensor_tensor", hbuf[:, b1 + 1:b1 + 1 + T],
               ub[:, c1 * T:(c1 + 1) * T], st_rstd[:], ALU.mult, inc=S["vsf"])
            nA1 = S["vsf"].n
            wait("v", S["vsf"], nA0)
            op("v", "tensor_tensor", hbuf[:, b0 + 1:b0 + 1 + T],
               hbuf[:, b0 + 1:b0 + 1 + T], st_mr[:], ALU.subtract,
               inc=S["vsf"])
            nB0 = S["vsf"].n
            wait("v", S["vsf"], nA1)
            op("v", "tensor_tensor", hbuf[:, b1 + 1:b1 + 1 + T],
               hbuf[:, b1 + 1:b1 + 1 + T], st_mr[:], ALU.subtract,
               inc=S["vsf"])
            nB1 = S["vsf"].n
            wait("v", S["vsf"], nB0)
            op("v", "tensor_tensor", dd[:], hbuf[:, b0 + 1:b0 + 1 + T],
               hbuf[:, b0:b0 + T], ALU.subtract, inc=S["vsf"])
            nD0 = S["vsf"].n
            wait("v", S["vsf"], nB1)
            op("v", "tensor_tensor", dd2[:], hbuf[:, b1 + 1:b1 + 1 + T],
               hbuf[:, b1:b1 + T], ALU.subtract, inc=S["vsf"])
            nD1 = S["vsf"].n
            wait("v", S["vsf"], nD0)
            op("v", "scalar_tensor_tensor", xr[:, c0 * T:(c0 + 1) * T],
               dd[:], mur_sb[:, c0:c0 + 1], hbuf[:, b0:b0 + T],
               ALU.mult, ALU.add)
            op("v", "scalar_tensor_tensor", xk[:, c0 * T:(c0 + 1) * T],
               dd[:], muk_sb[:, c0:c0 + 1], hbuf[:, b0:b0 + T],
               ALU.mult, ALU.add, inc=S["mixc"])
            wait("v", S["vsf"], nD1)
            op("v", "scalar_tensor_tensor", xr[:, c1 * T:(c1 + 1) * T],
               dd2[:], mur_sb[:, c1:c1 + 1], hbuf[:, b1:b1 + T],
               ALU.mult, ALU.add)
            op("v", "scalar_tensor_tensor", xk[:, c1 * T:(c1 + 1) * T],
               dd2[:], muk_sb[:, c1:c1 + 1], hbuf[:, b1:b1 + T],
               ALU.mult, ALU.add, inc=S["mixc"])

    def stage_r(w):
        g = g_of[("r", w)]
        g_k = g_of[("kf", w)]
        last = (w == NWL - 1)
        wait("t", S["wld"], 32 * ((g_k if last else g) + 1))
        if w >= 1:
            wait("s", S["owe"], 8 * (w - 1) + 8)
        if not last:
            wait("t", S["mixc"], 8 * w + 8)
            for m in range(NCH):
                bk = 3 + m % 2
                acquire("t", bk)
                for k in range(NCH):
                    op("t", "matmul", pb[bk][:, 0:512],
                       wslot[g % 2][:, k * D + 128 * m: k * D + 128 * (m + 1)],
                       xr[:, k * T:(k + 1) * T],
                       start=(k == 0), stop=(k == NCH - 1),
                       inc=(S["rmm"] if k == NCH - 1 else None))
                wait("s", S["rmm"], 8 * w + m + 1)
                op("s", "activation", r_w[:, m * T:(m + 1) * T],
                   pb[bk][:, 0:512], ACT.Sigmoid, bias=0.0, scale=1.0,
                   inc=S["rsg"])
                release(bk, S["rsg"], S["rsg"].n)
            wuse_rel[g] = (S["rmm"], 8 * w + 8)
            return
        if w >= 1:
            wait("v", S["owm"], 8 * (w - 1) + 8)
        rbanks = [0, 1, 7, 2]
        kbanks = [3, 4, 5, 6]
        for half in range(2):
            ms = [0, 1, 2, 3] if half == 0 else [4, 5, 6, 7]
            for mi in range(4):
                acquire("t", rbanks[mi])
                acquire("t", kbanks[mi])
            for k in range(NCH):
                if half == 0:
                    wait("t", S["mixc"], 8 * w + k + 1)
                for mi, m in enumerate(ms):
                    op("t", "matmul", pb[rbanks[mi]][:, 0:512],
                       wslot[g % 2][:, k * D + 128 * m: k * D + 128 * (m + 1)],
                       xr[:, k * T:(k + 1) * T],
                       start=(k == 0), stop=(k == NCH - 1),
                       inc=(S["rmm"] if k == NCH - 1 else None))
                    op("t", "matmul", pb[kbanks[mi]][:, 0:512],
                       wslot[g_k % 2][:, k * D + 128 * m: k * D + 128 * (m + 1)],
                       xk[:, k * T:(k + 1) * T],
                       start=(k == 0), stop=(k == NCH - 1),
                       inc=(S["kfm"] if k == NCH - 1 else None))
            for mi, m in enumerate(ms):
                wait("s", S["rmm"], 8 * w + 4 * half + mi + 1)
                op("s", "activation", r_w[:, m * T:(m + 1) * T],
                   pb[rbanks[mi]][:, 0:512], ACT.Sigmoid, bias=0.0, scale=1.0,
                   inc=S["rsg"])
                release(rbanks[mi], S["rsg"], S["rsg"].n)
                wait("s", S["kfm"], 8 * w + 4 * half + mi + 1)
                op("s", "activation", ka[:, m * T:(m + 1) * T],
                   pb[kbanks[mi]][:, 0:512], ACT.Relu, bias=0.0, scale=1.0,
                   inc=S["kfr"])
                release(kbanks[mi], S["kfr"], S["kfr"].n)
                wait("v", S["kfr"], 8 * w + 4 * half + mi + 1)
                op("v", "tensor_tensor", ka[:, m * T:(m + 1) * T],
                   ka[:, m * T:(m + 1) * T], ka[:, m * T:(m + 1) * T],
                   ALU.mult, inc=S["ka_s"])
        wuse_rel[g] = (S["rmm"], 8 * w + 8)
        wuse_rel[g_k] = (S["kfm"], 8 * w + 8)

    def stage_kf(w):
        if w == NWL - 1:
            return
        g = g_of[("kf", w)]
        wait("t", S["wld"], 32 * (g + 1))
        if w >= 1:
            wait("v", S["owm"], 8 * (w - 1) + 8)
        for m in range(NCH):
            bk = 5 + m % 2
            acquire("t", bk)
            for k in range(NCH):
                op("t", "matmul", pb[bk][:, 0:512],
                   wslot[g % 2][:, k * D + 128 * m: k * D + 128 * (m + 1)],
                   xk[:, k * T:(k + 1) * T],
                   start=(k == 0), stop=(k == NCH - 1),
                   inc=(S["kfm"] if k == NCH - 1 else None))
            wait("s", S["kfm"], 8 * w + m + 1)
            op("s", "activation", ka[:, m * T:(m + 1) * T],
               pb[bk][:, 0:512], ACT.Relu, bias=0.0, scale=1.0,
               inc=S["kfr"])
            release(bk, S["kfr"], S["kfr"].n)
            wait("v", S["kfr"], 8 * w + m + 1)
            op("v", "tensor_tensor", ka[:, m * T:(m + 1) * T],
               ka[:, m * T:(m + 1) * T], ka[:, m * T:(m + 1) * T],
               ALU.mult, inc=S["ka_s"])
        wuse_rel[g] = (S["kfm"], 8 * w + 8)

    def stage_ow(w):
        g = g_of[("ow", w)]
        ub = xw[w % 3]
        wait("t", S["wld"], 32 * (g + 1))
        wait("t", S["ka_s"], 8 * w + 8)
        wait("v", S["rsg"], 8 * w + 8)
        for m in range(NCH):
            bk = (3, 4, 5, 6)[m % 4]
            base = m * (T + 1)
            acquire("t", bk)
            for k in range(NCH):
                op("t", "matmul", pb[bk][:, 0:512],
                   wslot[g % 2][:, k * D + 128 * m: k * D + 128 * (m + 1)],
                   ka[:, k * T:(k + 1) * T],
                   start=(k == 0), stop=(k == NCH - 1),
                   inc=(S["owm"] if k == NCH - 1 else None))
            wait("v", S["owm"], 8 * w + m + 1)
            if m >= 1:
                wait("v", S["owe"], 8 * w + m)
            op("v", "tensor_tensor", dd[:], pb[bk][:, 0:512],
               r_w[:, m * T:(m + 1) * T], ALU.mult, inc=S["vsf"])
            release(bk, S["vsf"], S["vsf"].n)
            wait("v", S["vsf"], S["vsf"].n)
            op("v", "tensor_tensor", ub[:, m * T:(m + 1) * T], dd[:],
               hbuf[:, base + 1:base + 1 + T], ALU.add, inc=S["owe"])
        wuse_rel[g] = (S["owm"], 8 * w + 8)
        if w + 2 < NWL:
            emit_convert(w + 2)

    def stage_LN2s(w):
        ln_mms(w, 2)
        ln_part1(w, 2)
        ob = xw[w % 3]
        if w == NWL - 1:
            ln2_tail(w)
        for c in range(NCH):
            wait("y", S["out_s"], 8 * w + c + 1)
            dma("y", outT[c * 128:(c + 1) * 128, T * w:T * (w + 1)],
                ob[:, c * T:(c + 1) * T], S["sto"])
        if w + 2 < NWL:
            wait("y", S["sto"], 128 * (w + 1))
            emit_xw_load(w + 2)
            xw_loaded[0] += 1

    emitters = {"Q": stage_Q, "S": stage_S, "y": stage_y, "Wo": stage_Wo,
                "LN1s": stage_LN1s, "r": stage_r, "kf": stage_kf,
                "ow": stage_ow, "LN2s": stage_LN2s}

    cons_idx = [0]

    for (st_name, w) in stages:
        emitters[st_name](w)
        if st_name in tag_of:
            cons_idx[0] += 1
            nl = next_load[0]
            if nl < len(cons_order) and nl == cons_idx[0] + 1:
                tag, ww = cons_order[nl][0], cons_order[nl][1]
                emit_wload(tag_of[tag])
                next_load[0] += 1

    if NWL == 1 and dbg_store:
        wait("g", S["out_s"], 8)
        dma("g", outT[512:640, :], xr[:], S["sto"])
        dma("g", outT[640:768, :], xk[:], S["sto"])
        dma("g", outT[768:896, :], ka[:], S["sto"])
        dma("g", outT[896:1024, :], r_w[:], S["sto"])
    if NWL == 0:
        if do_a2:
            wait("g", S["cev"], 8 * NEP)
            dma("g", outT[0:128, :], ctx_bf[:, 0:N], S["sto"])
            dma("g", outT[128:256, :], ctx_bf[:, N:2 * N], S["sto"])
            dma("g", outT[256:384, 0:D], sd_acc[:], S["sto"])
            dma("g", outT[384:512, 0:8], sinv[:], S["sto"])
            dma("g", outT[512:640, 0:D], sd_hi[:], S["sto"])
            dma("g", outT[640:768, 0:D], sd_lo[:], S["sto"])
            dma("g", outT[768:896, 0:8], srec[:, 0:8], S["sto"])
        else:
            wait("g", S["kev"], NJ)
            wait("g", S["siv"], 1)
            dma("g", outT[0:128, :], kexp[:, 0:N], S["sto"])

    wait("y", S["sto"], S["sto"].n)

    with es:
        with nc.Block() as block:
            @block.tensor
            def _(e):
                for f in TT:
                    f(e)

            @block.vector
            def _(e):
                for f in VV:
                    f(e)

            @block.scalar
            def _(e):
                for f in SS:
                    f(e)

            @block.sync
            def _(e):
                for f in SY:
                    f(e)

            @block.gpsimd
            def _(e):
                for f in GP:
                    f(e)
    return nc


_NC_CACHE = None


def kernel(L0_attn_x, L0_attn_Wq, L0_attn_Wk, L0_attn_Wv, L0_attn_Wo,
           L0_attn_bo, L0_ffn_W_r, L0_ffn_W_k, L0_ffn_W_v, L0_ffn_mu_r,
           L0_ffn_mu_k):
    global _NC_CACHE
    x = np.asarray(L0_attn_x, np.float32)
    B = x.shape[0]
    if _NC_CACHE is None:
        _NC_CACHE = build_nc()
    nc = _NC_CACHE
    base = {
        "wq": np.ascontiguousarray(np.asarray(L0_attn_Wq, np.float32)),
        "wk": np.ascontiguousarray(np.asarray(L0_attn_Wk, np.float32)),
        "wv": np.ascontiguousarray(np.asarray(L0_attn_Wv, np.float32)),
        "wo": np.ascontiguousarray(np.asarray(L0_attn_Wo, np.float32)),
        "wrT": np.ascontiguousarray(np.asarray(L0_ffn_W_r, np.float32).T),
        "wkT": np.ascontiguousarray(np.asarray(L0_ffn_W_k, np.float32).T),
        "wvT": np.ascontiguousarray(np.asarray(L0_ffn_W_v, np.float32).T),
        "bo": np.ascontiguousarray(np.asarray(L0_attn_bo, np.float32)),
        "mur": np.ascontiguousarray(np.asarray(L0_ffn_mu_r, np.float32)),
        "muk": np.ascontiguousarray(np.asarray(L0_ffn_mu_k, np.float32)),
    }
    in_maps = []
    for b in range(B):
        m = dict(base)
        m["xT"] = np.ascontiguousarray(x[b].T)
        in_maps.append(m)
    res = run_bass_kernel_spmd(nc, in_maps, core_ids=list(range(B)))
    out = np.empty_like(x)
    for b in range(B):
        out[b] = res.results[b]["outT"].T
    return out
